# version 02 (final)
# baseline (speedup 1.0000x reference)
import math
import numpy as np
import ml_dtypes
import concourse.bass as bass
import concourse.mybir as mybir
from concourse.bass_utils import run_bass_kernel_spmd

F32 = mybir.dt.float32
BF16 = mybir.dt.bfloat16
AF = mybir.ActivationFunctionType
ALU = mybir.AluOpType

LN64 = math.log(64.0)
GC1 = 0.7978845608028654
GC2 = 0.044715

CONSTS_F32 = [0.0, 1.0, -0.5, 1e-5, 1.0 / 1024.0, LN64, GC1, -1.0, 0.5]


class Cfg:
    def __init__(self, ncores=8, T=2048, cap=1280, cloc=256, debug=False):
        self.ncores = ncores
        self.T = T
        self.TC = T // 128
        self.D = 1024
        self.DC = 8
        self.N0 = 4096
        self.E0 = 16
        self.F0 = 4096
        self.CAP = cap
        self.CLOC = cloc
        self.E1 = 16
        self.I1 = 512
        self.ED = 2048
        self.NS = 16
        self.debug = debug


class Prog:

    ENGINES = ["sync", "vector", "scalar", "tensor", "gpsimd"]
    NSEM = 8

    def __init__(self, nc):
        self.nc = nc
        self.counts = [0] * self.NSEM
        self.i = 0
        self.waits = []
        self.incs = []
        self.fns = []
        self.engs = []
        self.lines = []
        self.sems = None

    def op(self, eng, fn, inc=1):
        import sys as _sys
        f = _sys._getframe(1)
        lines = []
        while f is not None and len(lines) < 3:
            if "kernel.py" in f.f_code.co_filename:
                lines.append(f.f_lineno)
            f = f.f_back
        self.lines.append(lines)
        i = self.i
        if i == 0:
            self.waits.append(None)
        else:
            ps = (i - 1) % self.NSEM
            self.waits.append((ps, self.counts[ps]))
        s = i % self.NSEM
        self.counts[s] += inc
        self.incs.append((s, inc))
        self.fns.append(fn)
        self.engs.append(eng)
        self.i += 1

    def dma(self, eng, out, in_, **kw):
        self.op(eng, lambda e, o=out, i_=in_, kw=kw: e.dma_start(o, i_, **kw), 16)

    def mm(self, out, lhsT, rhs, start, stop):
        self.op("tensor", lambda e, o=out, l=lhsT, r=rhs, st=start, sp=stop:
                e.matmul(o, l, r, start=st, stop=sp))

    def trans(self, out, in_, ident):
        self.op("tensor", lambda e, o=out, i_=in_, d=ident: e.transpose(o, i_, d))

    def cp(self, eng, out, in_):
        if eng == "scalar":
            self.op(eng, lambda e, o=out, i_=in_: e.copy(o, i_))
        else:
            self.op(eng, lambda e, o=out, i_=in_: e.tensor_copy(o, i_))

    def ts(self, eng, out, in0, s1, op0, s2=None, op1=None):
        if s2 is None:
            self.op(eng, lambda e, o=out, i0=in0, a=s1, p0=op0:
                    e.tensor_single_scalar(o, i0, a, p0))
        else:
            self.op(eng, lambda e, o=out, i0=in0, a=s1, b=s2, p0=op0, p1=op1:
                    e.tensor_scalar(o, i0, a, b, p0, p1))

    def tt(self, eng, out, a, b, alu):
        self.op(eng, lambda e, o=out, x=a, y=b, p=alu: e.tensor_tensor(o, x, y, p))

    def stt(self, eng, out, in0, scalar, in1, op0, op1):
        self.op(eng, lambda e, o=out, i0=in0, s=scalar, i1=in1, p0=op0, p1=op1:
                e.scalar_tensor_tensor(o, i0, s, i1, p0, p1))

    def act(self, out, in_, func, bias=0.0, scale=1.0, accum_out=None):
        self.op("scalar", lambda e, o=out, i_=in_, f=func, b=bias, s=scale, a=accum_out:
                e.activation(o, i_, f, bias=b, scale=s, accum_out=a))

    def red(self, eng, out, in_, alu):
        self.op(eng, lambda e, o=out, i_=in_, p=alu:
                e.tensor_reduce(o, i_, mybir.AxisListType.X, p))

    def scan(self, out, d0, d1, init, op0, op1):
        self.op("vector", lambda e, o=out, a=d0, b=d1, i_=init, p0=op0, p1=op1:
                e.tensor_tensor_scan(o, a, b, i_, p0, p1))

    def memset(self, eng, ap, v):
        self.op(eng, lambda e, a=ap, vv=v: e.memset(a, vv))

    def vmax(self, out8, in_):
        self.op("vector", lambda e, o=out8, i_=in_: e.max(o, i_))

    def vmaxidx(self, outu8, in_max8, in_values):
        self.op("vector", lambda e, o=outu8, m=in_max8, i_=in_values: e.max_index(o, m, i_))

    def recip(self, out, in_):
        self.op("vector", lambda e, o=out, i_=in_: e.reciprocal(o, i_))

    def collective(self, kind, alu, groups, ins, outs):
        self.op("gpsimd", lambda e, k=kind, a=alu, g=groups, i_=ins, o=outs:
                e.collective_compute(k, a, replica_groups=g, ins=i_, outs=o), 1)

    def emit(self, block, limit=None):
        sems = self.sems
        n = self.i if limit is None else min(limit, self.i)
        final = [0] * self.NSEM
        for j in range(n):
            si, inc = self.incs[j]
            final[si] += inc
        self.final = final
        for engname in self.ENGINES:
            idxs = [j for j in range(n) if self.engs[j] == engname]
            if not idxs and engname != "sync":
                continue

            def body(eng, eng_idxs=idxs, engname=engname):
                for j in eng_idxs:
                    w = self.waits[j]
                    if w is not None:
                        eng.wait_ge(sems[w[0]], w[1])
                    try:
                        inst = self.fns[j](eng)
                    except Exception as ex:
                        raise RuntimeError(
                            f"op {j} on {engname} from kernel.py lines "
                            f"{self.lines[j]}: {ex}") from ex
                    si, inc = self.incs[j]
                    inst.then_inc(sems[si], inc)
                if engname == "sync":
                    for s in range(self.NSEM):
                        eng.wait_ge(sems[s], self.final[s])
            getattr(block, engname)(body)


def build_program(cfg: Cfg, limit=None):
    nc = bass.Bass("TRN2", target_bir_lowering=False)
    c = cfg
    T, TC, D, DC = c.T, c.TC, c.D, c.DC
    CLC = c.CLOC // 128
    SLC = c.E0 * CLC

    def din(name, shape, dtype=F32):
        return nc.dram_tensor(name, list(shape), dtype, kind="ExternalInput").ap()

    x_in = din("x", [T, D])
    KT = din("KT", [D, c.N0])
    V0 = din("V0", [c.N0, D])
    GW0T = din("gw0T", [D, c.E0])
    W1 = din("w1", [c.E0, D, c.F0])
    W2 = din("w2", [c.E0, c.F0, D])
    A_in = din("A", [c.ED, c.NS])
    Bm_in = din("Bm", [c.ED, c.NS])
    Cm_in = din("Cm", [c.NS, c.ED])
    Dv_in = din("Dv", [c.ED, 1])
    Wdt_in = din("Wdt", [c.ED, 1])
    CW_in = din("convw", [c.ED, 3])
    WinT = din("WinT", [D, 2 * c.ED])
    WoutT = din("WoutT", [c.ED, D])
    GW1T = din("gw1T", [D, c.E1])
    GB1 = din("gb1", [1, c.E1])
    We1 = din("We1", [c.E1, D, c.I1])
    Be1 = din("be1", [c.E1, c.I1])
    We2 = din("We2", [c.E1, c.I1, D])
    Be2 = din("be2", [c.E1, D])
    We3 = din("We3", [c.E1, D, c.I1])
    Be3 = din("be3", [c.E1, c.I1])
    Ws1 = din("ws1", [D, D])
    Bs1 = din("bs1", [1, D])
    Ws2 = din("ws2", [D, D])
    Bs2 = din("bs2", [1, D])
    Ws3 = din("ws3", [D, D])
    Bs3 = din("bs3", [1, D])

    identf_in = din("identf", [128, 128])
    identb_in = din("identb", [128, 128], BF16)
    iota256_in = din("iota256", [128, c.CLOC])
    iotacol_in = din("iotacol", [128, 1])
    onescol_in = din("onescol", [128, 1])
    onescolb_in = din("onescolb", [128, 1], BF16)
    onesrow_in = din("onesrow", [1, 128])
    onesrowb_in = din("onesrowb", [1, 128], BF16)
    eoffcol_in = din("eoffcol", [c.E0, 1])
    rankmask_in = din("rankmask", [c.ncores, 1])
    constf_in = din("constf", [128, len(CONSTS_F32)])

    out_d = nc.dram_tensor("out", [T, D], F32, kind="ExternalOutput").ap()
    attn_scr = nc.dram_tensor("attn_scr", [T, D], F32).ap()
    cc_in_d = nc.dram_tensor("cc_in", [1, c.E0], F32)
    cc_out_d = nc.dram_tensor("cc_out", [c.ncores, c.E0], F32)

    dbg = {}
    if c.debug:
        for nm, shape in [("d_xn", [T, D]), ("d_kidx", [1, T]), ("d_prob", [1, T]),
                          ("d_ffn0", [T, D]), ("d_x1", [T, D]),
                          ("d_m", [c.NS, T]), ("d_z", [c.ED, T]),
                          ("d_comb", [T, c.E1]), ("d_attn", [T, D])]:
            dbg[nm] = nc.dram_tensor(nm, shape, F32, kind="ExternalOutput").ap()

    p = Prog(nc)

    sb_base = (nc._sbuf_addr_for_side("left") + 31) // 32 * 32
    sb_cap = nc.sbuf_bytes_remaining - 64
    sb_free = [(sb_base, sb_cap)]
    sb_live = {}
    sb_serial = [0]

    def alloc(name, shape, dtype=F32):
        per_part = 1
        for d_ in shape[1:]:
            per_part *= d_
        size = (per_part * mybir.dt.size(dtype) + 31) // 32 * 32
        for i, (st, sz) in enumerate(sb_free):
            if sz >= size:
                sb_free[i] = (st + size, sz - size)
                if sb_free[i][1] == 0:
                    sb_free.pop(i)
                sb_serial[0] += 1
                t = nc.alloc_sbuf_tensor_at(f"s{sb_serial[0]}_{name}", list(shape),
                                            dtype, offset=st)
                sb_live[name] = (st, size)
                return t
        raise MemoryError(f"SBUF alloc failed for {name} size {size}")

    def free(*names):
        for name in names:
            st, size = sb_live.pop(name)
            sb_free.append((st, size))
        sb_free.sort()
        merged = []
        for st, sz in sb_free:
            if merged and merged[-1][0] + merged[-1][1] == st:
                merged[-1] = (merged[-1][0], merged[-1][1] + sz)
            else:
                merged.append((st, sz))
        sb_free[:] = [tuple(m) for m in merged]

    psum_live = {}
    psum_order = []

    def palloc(name, shape, dtype=F32):
        cm = nc.psum_tensor("p_" + name + f"_{sb_serial[0]}_{len(psum_order)}", list(shape), dtype)
        sb_serial[0] += 1
        t = cm.__enter__()
        psum_live[name] = cm
        psum_order.append(name)
        return t

    def pfree(*names):
        for name in sorted(names, key=lambda n: -psum_order.index(n)):
            psum_order.remove(name)
            psum_live.pop(name).__exit__(None, None, None)

    identf = alloc("identf", [128, 128])
    identb = alloc("identb", [128, 128], BF16)
    iota256 = alloc("iota256", [128, c.CLOC])
    iotacol = alloc("iotacol", [128, 1])
    onescol = alloc("onescol", [128, 1])
    onescolb = alloc("onescolb", [128, 1], BF16)
    onesrow = alloc("onesrow", [1, 128])
    onesrowb = alloc("onesrowb", [1, 128], BF16)
    eoffcol = alloc("eoffcol", [c.E0, 1])
    rankmask = alloc("rankmask", [c.ncores, 1])
    constf = alloc("constf", [128, len(CONSTS_F32)])
    for i, v in enumerate(CONSTS_F32):
        nc.const_aps.register(F32, v, constf[:, i:i + 1])

    for t, d in [(identf, identf_in), (identb, identb_in), (iota256, iota256_in),
                 (iotacol, iotacol_in), (onescol, onescol_in), (onescolb, onescolb_in),
                 (onesrow, onesrow_in), (onesrowb, onesrowb_in), (eoffcol, eoffcol_in),
                 (rankmask, rankmask_in), (constf, constf_in)]:
        p.dma("sync", t[:], d)

    def pe_t(out_sb, in_sb, k):
        pt = palloc("pe_t_ps", [128, k])
        p.trans(pt[:], in_sb, identf[0:k, 0:k])
        p.cp("vector", out_sb, pt[:])
        pfree("pe_t_ps")

    def emit_ln(xnT_t, src_get, add_into_acc=None, write_tok=None, chunk_cb=None):
        xb = alloc("ln_xb", [128, D])
        xc = alloc("ln_xc", [128, D])
        st = alloc("ln_st", [128, 8])
        trash = alloc("ln_trash", [128, D], BF16)
        xnst = alloc("ln_xnst", [128, D], BF16)
        if chunk_cb is not None:
            xnf = alloc("ln_xnf", [128, D])
        pT = palloc("ln_pT", [128, 128], BF16)
        for sc in range(TC):
            xap = src_get(sc, xb)
            p.red("vector", st[:, 0:1], xap, ALU.add)
            p.ts("vector", st[:, 1:2], st[:, 0:1], 1.0 / D, ALU.mult)
            p.ts("vector", xc[:], xap, st[:, 1:2], ALU.subtract)
            p.act(trash[:], xc[:], AF.Square, accum_out=st[:, 2:3])
            p.act(st[:, 3:4], st[:, 2:3], AF.Ln, bias=1e-5, scale=1.0 / D)
            p.act(st[:, 4:5], st[:, 3:4], AF.Exp, scale=-0.5)
            p.ts("vector", xnst[:], xc[:], st[:, 4:5], ALU.mult)
            if add_into_acc is not None:
                p.stt("vector", add_into_acc(sc), xc[:], st[:, 4:5], add_into_acc(sc),
                      ALU.mult, ALU.add)
            if write_tok is not None:
                p.cp("gpsimd", write_tok[:, sc, :], xnst[:])
            for dk in range(DC):
                p.trans(pT[:], xnst[:, dk * 128:(dk + 1) * 128], identb[:])
                p.cp("vector", xnT_t[:, dk, sc * 128:(sc + 1) * 128], pT[:])
            if chunk_cb is not None:
                p.ts("vector", xnf[:], xc[:], st[:, 4:5], ALU.mult)
                chunk_cb(sc, xnf)
        free("ln_xb", "ln_xc", "ln_st", "ln_trash", "ln_xnst")
        if chunk_cb is not None:
            free("ln_xnf")
        pfree("ln_pT")

    xnT = alloc("xnT", [128, DC, T], BF16)
    xn_tok = alloc("xn_tok", [128, TC, D], BF16)

    gwf = alloc("gwf", [128, DC, c.E0])
    p.dma("sync", gwf[:], GW0T.rearrange("(a p) e -> p a e", p=128))
    lg = alloc("lg", [128, c.E0])
    mx8 = alloc("mx8", [128, 8])
    miu = alloc("miu", [128, 8], mybir.dt.uint32)
    idx_cols = alloc("idx_cols", [128, TC])
    prob_cols = alloc("prob_cols", [128, TC])
    exs = alloc("exs", [128, c.E0])
    sms = alloc("sms", [128, 4])
    xtf = alloc("xtf", [128, 128])
    pgg = palloc("pgg", [128, c.E0])
    pgt = palloc("pgt", [128, 128])

    def l0_gate_cb(sc, xnf):
        for dk in range(DC):
            p.trans(pgt[:], xnf[:, dk * 128:(dk + 1) * 128], identf[:])
            p.cp("vector", xtf[:], pgt[:])
            p.mm(pgg[:], xtf[:], gwf[:, dk, :], start=(dk == 0), stop=(dk == DC - 1))
        p.cp("vector", lg[:], pgg[:])
        p.vmax(mx8[:], lg[:])
        p.vmaxidx(miu[:], mx8[:], lg[:])
        p.cp("vector", idx_cols[:, sc:sc + 1], miu[:, 0:1])
        p.ts("vector", sms[:, 0:1], mx8[:, 0:1], -1.0, ALU.mult)
        p.act(exs[:], lg[:], AF.Exp, bias=sms[:, 0:1])
        p.red("vector", sms[:, 1:2], exs[:], ALU.add)
        p.recip(prob_cols[:, sc:sc + 1], sms[:, 1:2])

    def l0_src(sc, xb):
        p.dma("sync", xb[:], x_in[sc * 128:(sc + 1) * 128, :])
        return xb[:]

    emit_ln(xnT, l0_src, write_tok=xn_tok, chunk_cb=l0_gate_cb)
    pfree("pgg", "pgt")
    free("gwf", "lg", "mx8", "miu", "exs", "sms", "xtf")

    if c.debug:
        tmp = alloc("dmp", [128, D])
        for sc in range(TC):
            p.cp("vector", tmp[:], xn_tok[:, sc, :])
            p.dma("sync", dbg["d_xn"][sc * 128:(sc + 1) * 128, :], tmp[:])
        free("dmp")

    NQ = 512
    nqc = NQ // 128
    NB = c.N0 // 512

    attnw = alloc("attnw", [128, nqc, c.N0], BF16)
    ktf = alloc("ktf", [128, DC, 512])
    ktb = alloc("ktb", [128, DC, 512], BF16)
    vf = alloc("vf", [128, 4, D])
    vb = alloc("vb", [128, 4, D], BF16)
    sqp = alloc("sqp", [128, nqc, NB])
    sqs = alloc("sqs", [128, 4])
    trash2 = alloc("trash2", [128, 512])
    ost = alloc("ost", [128, D])

    for q in range(T // NQ):
        pm1 = palloc("pm1", [128, 512])
        pTb = palloc("pTb", [128, 128], BF16)
        for nb in range(NB):
            p.dma("sync", ktf[:], KT.rearrange("(a p) n -> p a n", p=128)[:, :, nb * 512:(nb + 1) * 512])
            p.cp("gpsimd", ktb[:], ktf[:])
            for sq in range(nqc):
                s = q * nqc + sq
                for dk in range(DC):
                    p.mm(pm1[:], xnT[:, dk, s * 128:(s + 1) * 128], ktb[:, dk, :],
                         start=(dk == 0), stop=(dk == DC - 1))
                p.cp("vector", attnw[:, sq, nb * 512:(nb + 1) * 512], pm1[:])
                p.act(trash2[:], pm1[:], AF.Square, accum_out=sqp[:, sq, nb:nb + 1])
        for sq in range(nqc):
            p.red("vector", sqs[:, 0:1], sqp[:, sq, :], ALU.add)
            p.act(sqs[:, 1:2], sqs[:, 0:1], AF.Ln)
            p.act(sqs[:, 2:3], sqs[:, 1:2], AF.Exp, bias=LN64, scale=-0.5)
            p.act(attnw[:, sq, :], attnw[:, sq, :], AF.Gelu, scale=sqs[:, 2:3])
            for nk in range(c.N0 // 128):
                p.trans(pTb[:], attnw[:, sq, nk * 128:(nk + 1) * 128], identb[:])
                p.cp("vector", attnw[:, sq, nk * 128:(nk + 1) * 128], pTb[:])
        pfree("pm1", "pTb")
        pms = [palloc(f"pm2_{i}", [128, D]) for i in range(nqc)]
        for nb in range(NB):
            p.dma("sync", vf[:], V0.rearrange("(a p) d -> p a d", p=128)[:, nb * 4:(nb + 1) * 4, :])
            p.cp("gpsimd", vb[:], vf[:])
            for k4 in range(4):
                nk = nb * 4 + k4
                for sq in range(nqc):
                    for nh in range(2):
                        p.mm(pms[sq][:, nh * 512:(nh + 1) * 512],
                             attnw[:, sq, nk * 128:(nk + 1) * 128],
                             vb[:, k4, nh * 512:(nh + 1) * 512],
                             start=(nk == 0), stop=(nk == c.N0 // 128 - 1))
        for sq in range(nqc):
            p.cp("vector", ost[:], pms[sq][:])
            p.dma("sync", attn_scr[(q * nqc + sq) * 128:(q * nqc + sq + 1) * 128, :], ost[:])
            if c.debug:
                p.dma("sync", dbg["d_attn"][(q * nqc + sq) * 128:(q * nqc + sq + 1) * 128, :], ost[:])
        pfree(*[f"pm2_{i}" for i in range(nqc)])

    free("attnw", "ktf", "ktb", "vf", "vb", "sqp", "sqs", "trash2", "ost")

    free("xnT")

    rowsb = alloc("rowsb", [1, 2 * T])
    idx_row = rowsb[0:1, 0:T]
    prob_row = rowsb[0:1, T:2 * T]
    pr = palloc("pr", [1, 256])
    for sc in range(TC):
        p.mm(pr[0:1, 0:128], idx_cols[:, sc:sc + 1], identf[:], start=True, stop=True)
        p.cp("vector", idx_row[0:1, sc * 128:(sc + 1) * 128], pr[0:1, 0:128])
        p.mm(pr[0:1, 128:256], prob_cols[:, sc:sc + 1], identf[:], start=True, stop=True)
        p.cp("vector", prob_row[0:1, sc * 128:(sc + 1) * 128], pr[0:1, 128:256])
    pfree("pr")

    mt = alloc("mt", [c.E0, 5 * T])
    maskT = mt[0:c.E0, 0:T]
    cumT = mt[0:c.E0, T:2 * T]
    selT = mt[0:c.E0, 2 * T:3 * T]
    t1pT = mt[0:c.E0, 3 * T:4 * T]
    zerosT = mt[0:c.E0, 4 * T:5 * T]
    p.memset("vector", zerosT, 0.0)

    def bcast_idx(dst):
        pb = palloc("pb16", [c.E0, T])
        for nh in range(T // 512):
            p.mm(pb[:, nh * 512:(nh + 1) * 512], onesrow[0:1, 0:c.E0],
                 idx_row[0:1, nh * 512:(nh + 1) * 512], start=True, stop=True)
        p.cp("vector", dst, pb[:])
        pfree("pb16")
        p.ts("vector", dst, dst, iotacol[0:c.E0, 0:1], ALU.subtract, 0.0, ALU.is_equal)

    bcast_idx(maskT)
    p.scan(cumT, maskT, zerosT, 0.0, ALU.add, ALU.add)

    counts_col = alloc("counts_col", [c.E0, 1])
    p.cp("vector", counts_col[:], cumT[:, T - 1:T])
    pc = palloc("pc", [c.E0, 2 + c.E0])
    p.mm(pc[0:1, 2:2 + c.E0], counts_col[:], identf[0:c.E0, 0:c.E0], start=True, stop=True)
    crow = alloc("crow", [1, c.E0])
    p.cp("vector", crow[:], pc[0:1, 2:2 + c.E0])
    p.dma("sync", cc_in_d.ap(), crow[:])
    p.collective("AllGather", ALU.bypass, [list(range(c.ncores))],
                 [cc_in_d.ap().opt()], [cc_out_d.ap().opt()])
    ccg = alloc("ccg", [c.ncores, c.E0])
    p.dma("sync", ccg[:], cc_out_d.ap())
    offs = alloc("offs", [c.E0, 2])
    p.mm(pc[0:c.E0, 0:1], ccg[:], rankmask[:], start=True, stop=True)
    p.cp("vector", offs[:, 0:1], pc[0:c.E0, 0:1])
    pfree("pc")
    free("crow", "ccg", "counts_col")

    p.ts("vector", offs[:, 1:2], offs[:, 0:1], 1.0, ALU.subtract)
    grankT = maskT
    p.ts("vector", grankT, cumT, offs[:, 1:2], ALU.add)
    p.ts("vector", selT, grankT, float(c.CAP), ALU.is_lt)
    bcast_idx(t1pT)
    p.tt("vector", selT, selT, t1pT, ALU.mult)
    p.ts("vector", t1pT, cumT, eoffcol[:], ALU.add)
    p.tt("vector", t1pT, t1pT, selT, ALU.mult)
    pk = palloc("pk", [1, T])
    for nh in range(T // 512):
        p.mm(pk[0:1, nh * 512:(nh + 1) * 512], onescol[0:c.E0, 0:1],
             t1pT[:, nh * 512:(nh + 1) * 512], start=True, stop=True)
    kidx_row = idx_row
    p.ts("vector", kidx_row, pk[0:1, :], 1.0, ALU.subtract)
    pfree("pk")
    free("mt", "offs")

    kidx_cols = idx_cols
    pr2 = palloc("pr2", [128, 1])
    for sc in range(TC):
        p.mm(pr2[:], kidx_row[0:1, sc * 128:(sc + 1) * 128],
             onescol[0:1, 0:1], start=True, stop=True)
        p.cp("vector", kidx_cols[:, sc:sc + 1], pr2[:])
    pfree("pr2")

    if c.debug:
        p.dma("sync", dbg["d_kidx"], kidx_row)
        p.dma("sync", dbg["d_prob"], prob_row[:])

    eout = alloc("eout", [128, SLC, D], BF16)
    Pe = alloc("Pe", [128, TC, c.CLOC], BF16)
    XeT = alloc("XeT", [128, DC, c.CLOC], BF16)
    hT = alloc("hT", [128, c.F0 // 128, c.CLOC], BF16)
    w1f = alloc("w1f", [128, DC, 256])
    w1b = alloc("w1b", [128, DC, 256], BF16)
    w2f = alloc("w2f", [128, 2, D])
    w2b = alloc("w2b", [128, 2, D], BF16)
    g1 = alloc("g1", [128, 1024])
    g2 = alloc("g2", [128, 1024])

    for e in range(c.E0):
        pdx = palloc("pdx", [128, c.CLOC])
        for sc in range(TC):
            p.ts("vector", Pe[:, sc, :], iota256[:, 0:c.CLOC], kidx_cols[:, sc:sc + 1],
                 ALU.subtract, float(-e * c.CLOC), ALU.is_equal)
        for dk in range(DC):
            for sc in range(TC):
                p.mm(pdx[:], xn_tok[:, sc, dk * 128:(dk + 1) * 128], Pe[:, sc, :],
                     start=(sc == 0), stop=(sc == TC - 1))
            p.cp("vector", XeT[:, dk, :], pdx[:])
        pfree("pdx")

        phh = palloc("phh", [128, 1024])
        nfc = 2
        W = nfc * c.CLOC
        for fb in range(c.F0 // 256):
            p.dma("sync", w1f[:], W1[e].rearrange("(a p) f -> p a f", p=128)[:, :, fb * 256:(fb + 1) * 256])
            p.cp("gpsimd", w1b[:], w1f[:])
            for fc in range(nfc):
                for dk in range(DC):
                    p.mm(phh[:, fc * c.CLOC:(fc + 1) * c.CLOC],
                         w1b[:, dk, fc * 128:(fc + 1) * 128], XeT[:, dk, :],
                         start=(dk == 0), stop=(dk == DC - 1))
            a_ap = phh[:, 0:W]
            p.act(g1[:, 0:W], a_ap, AF.Square)
            p.tt("vector", g1[:, 0:W], g1[:, 0:W], a_ap, ALU.mult)
            p.stt("vector", g2[:, 0:W], g1[:, 0:W], GC2, a_ap, ALU.mult, ALU.add)
            p.act(g2[:, 0:W], g2[:, 0:W], AF.Tanh, scale=GC1)
            p.ts("vector", g2[:, 0:W], g2[:, 0:W], 0.5, ALU.mult, 0.5, ALU.add)
            p.tt("vector", g1[:, 0:W], g2[:, 0:W], a_ap, ALU.mult)
            for fc in range(nfc):
                p.cp("gpsimd", hT[:, fb * nfc + fc, :], g1[:, fc * c.CLOC:(fc + 1) * c.CLOC])
        pfree("phh")

        pox = palloc("pox", [128, CLC, D])
        for fb in range(c.F0 // 256):
            p.dma("sync", w2f[:], W2[e].rearrange("(a p) d -> p a d", p=128)[:, fb * 2:(fb + 1) * 2, :])
            p.cp("gpsimd", w2b[:], w2f[:])
            for k4 in range(2):
                fk = fb * 2 + k4
                for cc in range(CLC):
                    for nh in range(2):
                        p.mm(pox[:, cc, nh * 512:(nh + 1) * 512],
                             hT[:, fk, cc * 128:(cc + 1) * 128],
                             w2b[:, k4, nh * 512:(nh + 1) * 512],
                             start=(fk == 0), stop=(fk == c.F0 // 128 - 1))
        for cc in range(CLC):
            p.cp("vector", eout[:, e * CLC + cc, :], pox[:, cc, :])
        pfree("pox")

    free("Pe", "XeT", "hT", "w1f", "w1b", "w2f", "w2b", "g1", "g2", "xn_tok")

    acc = alloc("acc", [128, TC, D])
    tmpc = alloc("tmpc", [128, D])
    for sc in range(TC):
        p.dma("sync", acc[:, sc, :], x_in[sc * 128:(sc + 1) * 128, :])
        p.dma("sync", tmpc[:], attn_scr[sc * 128:(sc + 1) * 128, :])
        p.tt("vector", acc[:, sc, :], acc[:, sc, :], tmpc[:], ALU.add)
    free("tmpc")
    if c.debug:
        accsnap = alloc("accsnap", [128, TC, D])
        for sc in range(TC):
            p.cp("gpsimd", accsnap[:, sc, :], acc[:, sc, :])

    kb_bc = alloc("kb_bc", [128, T])
    pr_bc = alloc("pr_bc", [128, T])
    pbc = palloc("pbc", [128, 512])
    for nh in range(T // 512):
        p.mm(pbc[:], onesrow[0:1, :], kidx_row[0:1, nh * 512:(nh + 1) * 512],
             start=True, stop=True)
        p.cp("vector", kb_bc[:, nh * 512:(nh + 1) * 512], pbc[:])
        p.mm(pbc[:], onesrow[0:1, :], prob_row[0:1, nh * 512:(nh + 1) * 512],
             start=True, stop=True)
        p.cp("vector", pr_bc[:, nh * 512:(nh + 1) * 512], pbc[:])
    pfree("pbc")
    free("rowsb", "idx_cols", "prob_cols")

    GRP = 4
    ptg = alloc("ptg", [128, GRP, T], BF16)
    eqt = alloc("eqt", [128, T])
    pcb = palloc("pcb", [128, 512])
    for g in range(SLC // GRP):
        for j in range(GRP):
            q = g * GRP + j
            p.ts("vector", eqt[:], kb_bc[:], iotacol[:], ALU.subtract,
                 float(q * 128), ALU.is_equal)
            p.tt("vector", ptg[:, j, :], eqt[:], pr_bc[:], ALU.mult)
        for sc in range(TC):
            for nh in range(2):
                for j in range(GRP):
                    p.mm(pcb[:], ptg[:, j, sc * 128:(sc + 1) * 128],
                         eout[:, g * GRP + j, nh * 512:(nh + 1) * 512],
                         start=(j == 0), stop=(j == GRP - 1))
                p.tt("vector", acc[:, sc, nh * 512:(nh + 1) * 512],
                     acc[:, sc, nh * 512:(nh + 1) * 512], pcb[:], ALU.add)
    pfree("pcb")
    free("ptg", "eqt", "kb_bc", "pr_bc", "eout")

    if c.debug:
        tmpd = alloc("tmpd", [128, D])
        for sc in range(TC):
            p.tt("vector", tmpd[:], acc[:, sc, :], accsnap[:, sc, :], ALU.subtract)
            p.dma("sync", dbg["d_ffn0"][sc * 128:(sc + 1) * 128, :], tmpd[:])
            p.dma("sync", dbg["d_x1"][sc * 128:(sc + 1) * 128, :], acc[:, sc, :])
        free("tmpd", "accsnap")

    xnT = alloc("xnT1", [128, DC, T], BF16)

    gw1f = alloc("gw1f", [128, DC, c.E1])
    p.dma("sync", gw1f[:], GW1T.rearrange("(a p) e -> p a e", p=128))
    lg1_cols = alloc("lg1_cols", [128, TC, c.E1])
    xtf1 = alloc("xtf1", [128, 128])
    pgg1 = palloc("pgg1", [128, c.E1])
    pgt1 = palloc("pgt1", [128, 128])

    def l1_gate_cb(sc, xnf):
        for dk in range(DC):
            p.trans(pgt1[:], xnf[:, dk * 128:(dk + 1) * 128], identf[:])
            p.cp("vector", xtf1[:], pgt1[:])
            p.mm(pgg1[:], xtf1[:], gw1f[:, dk, :], start=(dk == 0), stop=(dk == DC - 1))
        p.cp("vector", lg1_cols[:, sc, :], pgg1[:])

    def l1_src(sc, xb):
        p.cp("gpsimd", xb[:], acc[:, sc, :])
        return xb[:]

    emit_ln(xnT, l1_src, add_into_acc=lambda sc: acc[:, sc, :], chunk_cb=l1_gate_cb)
    pfree("pgg1", "pgt1")
    free("gw1f", "xtf1")

    ED, NS = c.ED, c.NS
    EC = ED // 128
    NSEG = 4 if T >= 2048 else 2
    LH = T // NSEG

    ld16 = alloc("ld16", [NS, 128])
    wdtc = alloc("wdtc", [128, EC])
    p.dma("sync", ld16[:], Wdt_in.rearrange("(a p) o -> a (p o)", p=128))
    pe_t(wdtc[:], ld16[:], NS)
    dvc = alloc("dvc", [128, EC])
    p.dma("sync", ld16[:], Dv_in.rearrange("(a p) o -> a (p o)", p=128))
    pe_t(dvc[:], ld16[:], NS)
    free("ld16")

    bmc = alloc("bmc", [128, NS])
    Bp = alloc("Bp", [128, EC, NS], BF16)
    pal = palloc("pal", [1, NS])
    for ec in range(EC):
        p.dma("sync", bmc[:], Bm_in[ec * 128:(ec + 1) * 128, :])
        p.ts("vector", Bp[:, ec, :], bmc[:], wdtc[:, ec:ec + 1], ALU.mult)
        p.dma("sync", bmc[:], A_in[ec * 128:(ec + 1) * 128, :])
        p.mm(pal[:], wdtc[:, ec:ec + 1], bmc[:], start=(ec == 0), stop=(ec == EC - 1))
    arow = alloc("arow", [1, NS])
    p.cp("vector", arow[:], pal[0:1, :])
    pfree("pal")
    pac = palloc("pac", [NS, 1])
    p.mm(pac[:], arow[:], onescol[0:1, 0:1], start=True, stop=True)
    alphac = alloc("alphac", [NS, 1])
    p.cp("vector", alphac[:], pac[:])
    pfree("pac")
    free("bmc", "arow")

    cmf = alloc("cmf", [NS, ED])
    cmb = alloc("cmb", [NS, ED], BF16)
    p.dma("sync", cmf[:], Cm_in)
    p.cp("vector", cmb[:], cmf[:])
    free("cmf")
    cwc = alloc("cwc", [128, EC, 3])
    for ec in range(EC):
        p.dma("sync", cwc[:, ec, :], CW_in[ec * 128:(ec + 1) * 128, :])

    xcv = alloc("xcv", [128, EC, LH + 2], BF16)
    xsv = alloc("xsv", [128, EC, LH], BF16)
    winf = alloc("winf", [128, DC, 128])
    winb = alloc("winb", [128, DC, 128], BF16)
    dtrow = alloc("dtrow", [1, LH])
    dtb = alloc("dtb", [NS, LH])
    aT = alloc("aT", [NS, LH])
    uT = alloc("uT", [NS, LH])
    mT = alloc("mT", [NS, LH])
    mTb = alloc("mTb", [NS, LH], BF16)
    mcar = alloc("mcar", [NS, 1])
    yT = alloc("yT", [128, LH])
    gT = alloc("gT", [128, LH])
    cvo = alloc("cvo", [128, LH])
    woutf = alloc("woutf", [128, D])
    woutb = alloc("woutb", [128, D], BF16)

    p.memset("vector", mcar[:], 0.0)

    for half in range(NSEG):
        l0_ = half * LH
        if half > 0:
            for ec in range(EC):
                p.cp("gpsimd", xcv[:, ec, 0:1], xcv[:, ec, LH:LH + 1])
        pproj = palloc("pproj", [128, 512])
        for part in range(2):
            dst = xcv if part == 0 else xsv
            off = 1 if part == 0 else 0
            for ec in range(EC):
                gec = part * EC + ec
                p.dma("sync", winf[:], WinT.rearrange("(a p) e -> p a e", p=128)[:, :, gec * 128:(gec + 1) * 128])
                p.cp("gpsimd", winb[:], winf[:])
                for nh in range(LH // 512):
                    for dk in range(DC):
                        p.mm(pproj[:], winb[:, dk, :],
                             xnT[:, dk, l0_ + nh * 512:l0_ + (nh + 1) * 512],
                             start=(dk == 0), stop=(dk == DC - 1))
                    p.cp("vector", dst[:, ec, off + nh * 512:off + (nh + 1) * 512], pproj[:])
                if part == 0:
                    if half < NSEG - 1:
                        for dk in range(DC):
                            p.mm(pproj[:, 0:1], winb[:, dk, :],
                                 xnT[:, dk, l0_ + LH:l0_ + LH + 1],
                                 start=(dk == 0), stop=(dk == DC - 1))
                        p.cp("vector", xcv[:, ec, LH + 1:LH + 2], pproj[:, 0:1])
                    else:
                        p.memset("vector", xcv[:, ec, LH + 1:LH + 2], 0.0)
                    if half == 0:
                        p.memset("vector", xcv[:, ec, 0:1], 0.0)
        pfree("pproj")

        pdt = palloc("pdt", [1, LH])
        for nh in range(LH // 512):
            for ec in range(EC):
                p.mm(pdt[0:1, nh * 512:(nh + 1) * 512], onescolb[:],
                     xsv[:, ec, nh * 512:(nh + 1) * 512],
                     start=(ec == 0), stop=(ec == EC - 1))
        p.ts("vector", dtrow[:], pdt[0:1, :], 1.0 / ED, ALU.mult)
        pfree("pdt")
        pdb = palloc("pdb", [NS, LH])
        for nh in range(LH // 512):
            p.mm(pdb[:, nh * 512:(nh + 1) * 512], onesrow[0:1, 0:NS],
                 dtrow[0:1, nh * 512:(nh + 1) * 512], start=True, stop=True)
        p.cp("vector", dtb[:], pdb[:])
        pfree("pdb")
        pu = palloc("pu", [NS, LH])
        for nh in range(LH // 512):
            for ec in range(EC):
                p.mm(pu[:, nh * 512:(nh + 1) * 512], Bp[:, ec, :],
                     xsv[:, ec, nh * 512:(nh + 1) * 512],
                     start=(ec == 0), stop=(ec == EC - 1))
        p.tt("vector", uT[:], pu[:], dtb[:], ALU.mult)
        pfree("pu")
        p.ts("vector", aT[:], dtb[:], alphac[:], ALU.mult)
        p.scan(mT[:], aT[:], uT[:], mcar[:], ALU.mult, ALU.add)
        p.cp("vector", mcar[:], mT[:, LH - 1:LH])
        p.cp("vector", mTb[:], mT[:])
        if c.debug:
            p.dma("sync", dbg["d_m"][:, l0_:l0_ + LH], mT[:])

        py = palloc("py", [128, LH])
        for ec in range(EC):
            for nh in range(LH // 512):
                p.mm(py[:, nh * 512:(nh + 1) * 512], cmb[0:NS, ec * 128:(ec + 1) * 128],
                     mTb[0:NS, nh * 512:(nh + 1) * 512], start=True, stop=True)
            p.stt("vector", yT[:], xsv[:, ec, :], dvc[:, ec:ec + 1], py[:],
                  ALU.mult, ALU.add)
            p.act(gT[:], yT[:], AF.Sigmoid)
            p.ts("vector", cvo[:], xcv[:, ec, 1:LH + 1], cwc[:, ec, 1:2], ALU.mult)
            p.stt("vector", cvo[:], xcv[:, ec, 0:LH], cwc[:, ec, 0:1], cvo[:],
                  ALU.mult, ALU.add)
            p.stt("vector", cvo[:], xcv[:, ec, 2:LH + 2], cwc[:, ec, 2:3], cvo[:],
                  ALU.mult, ALU.add)
            p.tt("vector", cvo[:], cvo[:], yT[:], ALU.subtract)
            p.tt("vector", cvo[:], cvo[:], gT[:], ALU.mult)
            p.tt("vector", cvo[:], cvo[:], yT[:], ALU.add)
            p.cp("vector", xsv[:, ec, :], cvo[:])
            if c.debug:
                p.dma("sync", dbg["d_z"][ec * 128:(ec + 1) * 128, l0_:l0_ + LH], cvo[:])
        pfree("py")

        pw = [palloc(f"pw{i}", [128, D]) for i in range(4)]
        for grp in range(LH // 512):
            for ec in range(EC):
                p.dma("sync", woutf[:], WoutT[ec * 128:(ec + 1) * 128, :])
                p.cp("gpsimd", woutb[:], woutf[:])
                for lc in range(4):
                    for nh in range(2):
                        p.mm(pw[lc][:, nh * 512:(nh + 1) * 512],
                             xsv[:, ec, grp * 512 + lc * 128:grp * 512 + (lc + 1) * 128],
                             woutb[:, nh * 512:(nh + 1) * 512],
                             start=(ec == 0), stop=(ec == EC - 1))
            for lc in range(4):
                sc = (l0_ + grp * 512 + lc * 128) // 128
                p.tt("vector", acc[:, sc, :], acc[:, sc, :], pw[lc][:], ALU.add)
        pfree(*[f"pw{i}" for i in range(4)])

    free("wdtc", "Bp", "alphac", "cmb", "dvc", "cwc", "xcv", "xsv", "winf",
         "winb", "dtrow", "dtb", "aT", "uT", "mT", "mTb", "mcar",
         "yT", "gT", "cvo", "woutf", "woutb")

    E1, I1 = c.E1, c.I1
    gbrow = alloc("gbrow", [1, E1])
    p.dma("sync", gbrow[:], GB1)
    gb_bc = alloc("gb_bc", [128, E1])
    pgb = palloc("pgb", [128, E1])
    p.mm(pgb[:], onesrow[0:1, :], gbrow[:], start=True, stop=True)
    p.cp("vector", gb_bc[:], pgb[:])
    pfree("pgb")
    free("gbrow")

    comb = alloc("comb", [128, TC, E1])
    lg1 = alloc("lg1", [128, E1])
    orig = alloc("orig", [128, E1])
    sb4 = alloc("sb4", [128, 4, 4])
    eq4 = alloc("eq4", [128, 4, 4])
    gsc = alloc("gsc", [128, 4])
    gsc2 = alloc("gsc2", [128, 4])
    kbv = alloc("kbv", [128, 4])
    st1 = alloc("st1", [128, 8])
    mx8b = alloc("mx8b", [128, 8])
    eqe = alloc("eqe", [128, E1])
    sb4f = sb4[:].rearrange("p a b -> p (a b)")
    eq4f = eq4[:].rearrange("p a b -> p (a b)")
    for sc in range(TC):
        p.cp("vector", lg1[:], lg1_cols[:, sc, :])
        p.red("vector", st1[:, 0:1], lg1[:], ALU.max)
        p.ts("vector", st1[:, 1:2], st1[:, 0:1], -1.0, ALU.mult)
        p.act(orig[:], lg1[:], AF.Exp, bias=st1[:, 1:2])
        p.red("vector", st1[:, 2:3], orig[:], ALU.add)
        p.recip(st1[:, 3:4], st1[:, 2:3])
        p.ts("vector", orig[:], orig[:], st1[:, 3:4], ALU.mult)
        p.tt("vector", sb4f, orig[:], gb_bc[:], ALU.add)
        p.red("vector", gsc[:], sb4[:], ALU.max)
        for g in range(4):
            p.ts("vector", eq4[:, g, :], sb4[:, g, :], gsc[:, g:g + 1], ALU.is_equal)
        p.stt("vector", eq4f, eq4f, -1e30, sb4f, ALU.mult, ALU.add)
        p.red("vector", gsc2[:], eq4[:], ALU.max)
        p.tt("vector", gsc[:], gsc[:], gsc2[:], ALU.add)
        p.red("vector", st1[:, 4:5], gsc[:], ALU.max)
        p.ts("vector", gsc2[:], gsc[:], st1[:, 4:5], ALU.is_equal)
        p.stt("vector", gsc2[:], gsc2[:], -1e30, gsc[:], ALU.mult, ALU.add)
        p.red("vector", st1[:, 5:6], gsc2[:], ALU.max)
        p.ts("vector", gsc[:], gsc[:], st1[:, 5:6], ALU.is_ge)
        p.ts("vector", kbv[:], gsc[:], 1.0, ALU.subtract, 1e30, ALU.mult)
        for g in range(4):
            p.ts("vector", sb4[:, g, :], sb4[:, g, :], gsc[:, g:g + 1], ALU.mult,
                 kbv[:, g:g + 1], ALU.add)
        p.vmax(mx8b[:], sb4f)
        p.ts("vector", eqe[:], sb4f, mx8b[:, 0:1], ALU.is_equal)
        p.ts("vector", lg1[:], sb4f, mx8b[:, 1:2], ALU.is_equal)
        p.tt("vector", eqe[:], eqe[:], lg1[:], ALU.add)
        p.tt("vector", comb[:, sc, :], orig[:], eqe[:], ALU.mult)
        if c.debug:
            p.dma("sync", dbg["d_comb"][sc * 128:(sc + 1) * 128, :], comb[:, sc, :])
    free("lg1", "orig", "sb4", "eq4", "gsc", "gsc2", "kbv", "st1", "mx8b",
         "eqe", "gb_bc", "lg1_cols")

    be2f = alloc("be2f", [E1, D])
    be2b = alloc("be2b", [E1, D], BF16)
    p.dma("sync", be2f[:], Be2)
    p.cp("vector", be2b[:], be2f[:])
    free("be2f")
    cbT = alloc("cbT", [E1, 128], BF16)
    cbTf = alloc("cbTf", [E1, 128])
    pct = palloc("pct", [E1, 128])
    pbe = palloc("pbe", [128, D])
    for sc in range(TC):
        p.trans(pct[:], comb[:, sc, :], identf[:])
        p.cp("vector", cbTf[:], pct[:])
        p.cp("vector", cbT[:], cbTf[:])
        for nh in range(2):
            p.mm(pbe[:, nh * 512:(nh + 1) * 512], cbT[:],
                 be2b[0:E1, nh * 512:(nh + 1) * 512], start=True, stop=True)
        p.tt("vector", acc[:, sc, :], acc[:, sc, :], pbe[:], ALU.add)
    pfree("pct", "pbe")
    free("cbT", "cbTf", "be2b")

    NSI = 2 * I1
    hsT = alloc("hsT", [128, NSI // 128, T], BF16)
    ws1f = alloc("ws1f", [128, DC, 128])
    ws3f = alloc("ws3f", [128, DC, 128])
    ws1b = alloc("ws1b", [128, DC, 128], BF16)
    ws3b = alloc("ws3b", [128, DC, 128], BF16)
    brow = alloc("brow", [8, 128])
    bs1c = alloc("bs1c", [128, NSI // 128])
    bs3c = alloc("bs3c", [128, NSI // 128])
    p.dma("sync", brow[:], Bs1.rearrange("o (a p) -> (o a) p", p=128))
    pe_t(bs1c[:], brow[:], 8)
    p.dma("sync", brow[:], Bs3.rearrange("o (a p) -> (o a) p", p=128))
    pe_t(bs3c[:], brow[:], 8)
    free("brow")
    s1buf = alloc("s1buf", [128, 512])
    s3buf = alloc("s3buf", [128, 512])
    ps1 = palloc("ps1", [128, 512])
    ps3 = palloc("ps3", [128, 512])
    for ic in range(NSI // 128):
        p.dma("sync", ws1f[:], Ws1.rearrange("(a p) i -> p a i", p=128)[:, :, ic * 128:(ic + 1) * 128])
        p.dma("sync", ws3f[:], Ws3.rearrange("(a p) i -> p a i", p=128)[:, :, ic * 128:(ic + 1) * 128])
        p.cp("gpsimd", ws1b[:], ws1f[:])
        p.cp("gpsimd", ws3b[:], ws3f[:])
        for nh in range(T // 512):
            for dk in range(DC):
                p.mm(ps1[:], ws1b[:, dk, :], xnT[:, dk, nh * 512:(nh + 1) * 512],
                     start=(dk == 0), stop=(dk == DC - 1))
            for dk in range(DC):
                p.mm(ps3[:], ws3b[:, dk, :], xnT[:, dk, nh * 512:(nh + 1) * 512],
                     start=(dk == 0), stop=(dk == DC - 1))
            p.act(s1buf[:], ps1[:], AF.Silu, bias=bs1c[:, ic:ic + 1])
            p.ts("vector", s3buf[:], ps3[:], bs3c[:, ic:ic + 1], ALU.add)
            p.tt("vector", s1buf[:], s1buf[:], s3buf[:], ALU.mult)
            p.cp("vector", hsT[:, ic, nh * 512:(nh + 1) * 512], s1buf[:])
    pfree("ps1", "ps3")
    free("ws1f", "ws3f", "ws1b", "ws3b", "bs1c", "bs3c", "s1buf", "s3buf")

    ws2f = alloc("ws2f", [128, D])
    ws2b = alloc("ws2b", [128, NSI // 128, D], BF16)
    for ic in range(NSI // 128):
        p.dma("sync", ws2f[:], Ws2[ic * 128:(ic + 1) * 128, :])
        p.cp("gpsimd", ws2b[:, ic, :], ws2f[:])
    bs2row = alloc("bs2row", [1, D])
    bs2rb = alloc("bs2rb", [1, D], BF16)
    p.dma("sync", bs2row[:], Bs2)
    p.cp("vector", bs2rb[:], bs2row[:])
    pz = palloc("pz", [128, D])
    for sc in range(TC):
        for ic in range(NSI // 128):
            for nh in range(2):
                p.mm(pz[:, nh * 512:(nh + 1) * 512],
                     hsT[:, ic, sc * 128:(sc + 1) * 128],
                     ws2b[:, ic, nh * 512:(nh + 1) * 512],
                     start=(ic == 0), stop=False)
        for nh in range(2):
            p.mm(pz[:, nh * 512:(nh + 1) * 512], onesrowb[0:1, 0:128],
                 bs2rb[0:1, nh * 512:(nh + 1) * 512], start=False, stop=True)
        p.tt("vector", acc[:, sc, :], acc[:, sc, :], pz[:], ALU.add)
    pfree("pz")
    free("ws2f", "ws2b", "bs2row", "bs2rb", "hsT")

    h1T = alloc("h1T", [128, I1 // 128, T], BF16)
    we1f = alloc("we1f", [128, DC, 256])
    we1b = alloc("we1b", [128, DC, I1], BF16)
    we3b = alloc("we3b", [128, DC, I1], BF16)
    we2f = alloc("we2f", [128, 2, D])
    we2b = alloc("we2b", [128, 4, D], BF16)
    brow4 = alloc("brow4", [4, 128])
    be1c = alloc("be1c", [128, I1 // 128])
    be3c = alloc("be3c", [128, I1 // 128])
    e1buf = alloc("e1buf", [128, 512])
    e3buf = alloc("e3buf", [128, 512])
    pe1 = palloc("pe1", [128, 512])
    pe3 = palloc("pe3", [128, 512])
    pyo = palloc("pyo", [128, D])
    for e in range(E1):
        for ih in range(2):
            p.dma("sync", we1f[:], We1[e].rearrange("(a p) i -> p a i", p=128)[:, :, ih * 256:(ih + 1) * 256])
            p.cp("gpsimd", we1b[:, :, ih * 256:(ih + 1) * 256], we1f[:])
            p.dma("sync", we1f[:], We3[e].rearrange("(a p) i -> p a i", p=128)[:, :, ih * 256:(ih + 1) * 256])
            p.cp("gpsimd", we3b[:, :, ih * 256:(ih + 1) * 256], we1f[:])
        p.dma("sync", brow4[:], Be1[e].rearrange("(a p) -> a p", p=128))
        pe_t(be1c[:], brow4[:], 4)
        p.dma("sync", brow4[:], Be3[e].rearrange("(a p) -> a p", p=128))
        pe_t(be3c[:], brow4[:], 4)
        for ic in range(I1 // 128):
            for nh in range(T // 512):
                for dk in range(DC):
                    p.mm(pe1[:], we1b[:, dk, ic * 128:(ic + 1) * 128],
                         xnT[:, dk, nh * 512:(nh + 1) * 512],
                         start=(dk == 0), stop=(dk == DC - 1))
                for dk in range(DC):
                    p.mm(pe3[:], we3b[:, dk, ic * 128:(ic + 1) * 128],
                         xnT[:, dk, nh * 512:(nh + 1) * 512],
                         start=(dk == 0), stop=(dk == DC - 1))
                p.act(e1buf[:], pe1[:], AF.Silu, bias=be1c[:, ic:ic + 1])
                p.ts("vector", e3buf[:], pe3[:], be3c[:, ic:ic + 1], ALU.add)
                p.tt("vector", e1buf[:], e1buf[:], e3buf[:], ALU.mult)
                p.cp("vector", h1T[:, ic, nh * 512:(nh + 1) * 512], e1buf[:])
        for ih in range(2):
            p.dma("sync", we2f[:], We2[e].rearrange("(a p) d -> p a d", p=128)[:, ih * 2:(ih + 1) * 2, :])
            p.cp("gpsimd", we2b[:, ih * 2:(ih + 1) * 2, :], we2f[:])
        for sc in range(TC):
            for ic in range(I1 // 128):
                for nh in range(2):
                    p.mm(pyo[:, nh * 512:(nh + 1) * 512],
                         h1T[:, ic, sc * 128:(sc + 1) * 128],
                         we2b[:, ic, nh * 512:(nh + 1) * 512],
                         start=(ic == 0), stop=(ic == I1 // 128 - 1))
            p.stt("vector", acc[:, sc, :], pyo[:], comb[:, sc, e:e + 1],
                  acc[:, sc, :], ALU.mult, ALU.add)
    pfree("pe1", "pe3", "pyo")
    free("h1T", "we1f", "we1b", "we3b", "we2f", "we2b", "brow4", "be1c", "be3c",
         "e1buf", "e3buf", "comb", "xnT1")

    for sc in range(TC):
        p.dma("sync", out_d[sc * 128:(sc + 1) * 128, :], acc[:, sc, :])

    sem_cms = [nc.semaphore(f"chain{i}") for i in range(Prog.NSEM)]
    p.sems = [cm.__enter__() for cm in sem_cms]
    block_cm = nc.Block()
    block = block_cm.__enter__()
    p.emit(block, limit=limit)
    block_cm.__exit__(None, None, None)
    for cm in sem_cms:
        cm.__exit__(None, None, None)
    nc.finalize()
    return nc


def make_aux(cfg: Cfg, core_id: int):
    c = cfg
    aux = {}
    aux["identf"] = np.eye(128, dtype=np.float32)
    aux["identb"] = np.eye(128, dtype=np.float32).astype(ml_dtypes.bfloat16)
    aux["iota256"] = np.tile(np.arange(c.CLOC, dtype=np.float32), (128, 1))
    aux["iotacol"] = np.arange(128, dtype=np.float32).reshape(128, 1)
    aux["onescol"] = np.ones((128, 1), np.float32)
    aux["onescolb"] = np.ones((128, 1), np.float32).astype(ml_dtypes.bfloat16)
    aux["onesrow"] = np.ones((1, 128), np.float32)
    aux["onesrowb"] = np.ones((1, 128), np.float32).astype(ml_dtypes.bfloat16)
    aux["eoffcol"] = (np.arange(c.E0, dtype=np.float32) * c.CLOC).reshape(c.E0, 1)
    aux["rankmask"] = (np.arange(c.ncores) < core_id).astype(np.float32).reshape(c.ncores, 1)
    aux["constf"] = np.tile(np.array(CONSTS_F32, np.float32), (128, 1))
    return aux


def make_in_maps(cfg, inputs):
    c = cfg
    f = lambda a: np.ascontiguousarray(np.asarray(a, dtype=np.float32))
    w = {
        "KT": f(np.asarray(inputs["L0_attn_key_param_tokens"]).T),
        "V0": f(inputs["L0_attn_value_param_tokens"]),
        "gw0T": f(np.asarray(inputs["L0_ffn_gate_w"]).T),
        "w1": f(inputs["L0_ffn_w1"]),
        "w2": f(inputs["L0_ffn_w2"]),
        "A": f(inputs["L1_attn_A"]),
        "Bm": f(inputs["L1_attn_Bm"]),
        "Cm": f(inputs["L1_attn_Cm"]),
        "Dv": f(np.asarray(inputs["L1_attn_Dv"]).reshape(-1, 1)),
        "Wdt": f(inputs["L1_attn_W_dt"]),
        "convw": f(np.asarray(inputs["L1_attn_conv_w"]).reshape(c.ED, 3)),
        "WinT": f(np.asarray(inputs["L1_attn_W_in"]).T),
        "WoutT": f(np.asarray(inputs["L1_attn_W_out"]).T),
        "gw1T": f(np.asarray(inputs["L1_ffn_gate_w"]).T),
        "gb1": f(np.asarray(inputs["L1_ffn_gate_b"]).reshape(1, -1)),
        "We1": f(inputs["L1_ffn_We1"]),
        "be1": f(inputs["L1_ffn_be1"]),
        "We2": f(inputs["L1_ffn_We2"]),
        "be2": f(inputs["L1_ffn_be2"]),
        "We3": f(inputs["L1_ffn_We3"]),
        "be3": f(inputs["L1_ffn_be3"]),
        "ws1": f(inputs["L1_ffn_ws1"]),
        "bs1": f(np.asarray(inputs["L1_ffn_bs1"]).reshape(1, -1)),
        "ws2": f(inputs["L1_ffn_ws2"]),
        "bs2": f(np.asarray(inputs["L1_ffn_bs2"]).reshape(1, -1)),
        "ws3": f(inputs["L1_ffn_ws3"]),
        "bs3": f(np.asarray(inputs["L1_ffn_bs3"]).reshape(1, -1)),
    }
    x = np.asarray(inputs["L0_attn_inputs"])
    in_maps = []
    for core in range(c.ncores):
        m = dict(w)
        m["x"] = f(x[core])
        m.update(make_aux(c, core))
        in_maps.append(m)
    return in_maps


def kernel(**inputs):
    cfg = Cfg()
    nc = build_program(cfg)
    in_maps = make_in_maps(cfg, inputs)
    res = run_bass_kernel_spmd(nc, in_maps, core_ids=list(range(cfg.ncores)))
    out = np.stack([np.asarray(r["out"], dtype=np.float32) for r in res.results], axis=0)
    return out
